# version 32
# baseline (speedup 1.0000x reference)
import numpy as np
import concourse.bass_utils as _bu
_orig_run_command = _bu.run_command
def _ldwopt_run_command(argv, **kwargs):
    argv = [a.replace("--enable-ldw-opt=false", "--enable-ldw-opt=true")
            if isinstance(a, str) else a for a in argv]
    return _orig_run_command(argv, **kwargs)
_bu.run_command = _ldwopt_run_command
import concourse.bass as bass
import concourse.mybir as mybir
from concourse.bass_utils import run_bass_kernel_spmd

F32 = mybir.dt.float32
BF16 = mybir.dt.bfloat16
AF = mybir.ActivationFunctionType
ALU = mybir.AluOpType
AXX = mybir.AxisListType.X

D = 1024
T = 2048
E = 8
FF = 2048
NDT = 8
NFT = 16
NTH = 2
EPS = 1e-5

ENGS = ("pe", "ve", "sc", "gp", "sy")


class Walk:

    def __init__(self, eng, q, nc, tn, sems):
        self.eng = eng
        self.q = q
        self.nc = nc
        self.t = tn
        self.S = sems
        self.c = {e: 0 for e in ENGS}
        self.gpc = 0
        self.d = {}
        self.M = {}

    def mark(self, engname, inst, key=None):
        self.c[engname] += 1
        if key is not None:
            self.M[key] = ("s_" + engname, self.c[engname])
        if self.eng == engname and inst is not None:
            inst.then_inc(self.S["s_" + engname], 1)

    def dmark(self, semname, inst, key=None):
        self.d[semname] = self.d.get(semname, 0) + 16
        if key is not None:
            self.M[key] = (semname, self.d[semname])
        if self.eng in ("gp", "sy") and inst is not None:
            inst.then_inc(self.S[semname], 16)

    def wt(self, engname, key, opt=False):
        if key not in self.M:
            if not opt:
                raise KeyError("missing milestone %r" % (key,))
            return
        if self.eng != engname:
            return
        nm, val = self.M[key]
        self.q.wait_ge(self.S[nm], val)

    def dr(self, engname):
        if self.eng == engname:
            self.q.drain()

    def sep(self, inst=None):
        if self.eng == "ve":
            self.q.drain()
        elif self.eng == "gp":
            if inst is not None:
                inst.then_inc(self.S["s_gpc"], 1)
            self.gpc += 1
            self.q.wait_ge(self.S["s_gpc"], self.gpc)

    def on(self, engname):
        return self.eng == engname

    def run(self):
        self.setup()
        self.ffn_pass(0)
        self.attn()
        self.ffn_pass(1)
        self.output()

    def setup(self):
        t = self.t
        pairs = [
            (t["ident_s"][:, :], t["d_ident"][:, :]),
            (t["iota_s"][:, :], t["d_iota"][:, :]),
            (t["gm_s"][:, :], t["d_gmask"][:, :]),
            (t["onesr_s"][:, :], t["d_onesr"][:, :]),
            (t["eps_s"][:, :], t["d_eps"][:, :]),
            (t["av_s"][:, :], t["d_av"][:, :]),
            (t["bv_s"][:, :], t["d_bv"][:, :]),
            (t["bo_s"][:, :], t["d_bo"][:, :]),
            (t["onescf"][:, :], t["d_onesc"][:, :]),
        ]
        for dst, src in pairs:
            inst = self.q.dma_start(dst, src) if self.on("sy") else None
            self.dmark("d_in", inst)
        self.M["in_all"] = ("d_in", self.d["d_in"])
        agz_src = bass.AP(t["d_gmask"], 0, [[0, 1], [1, 1024]])
        inst = self.q.dma_start(t["d_agin"][:, :], agz_src) if self.on("sy") else None
        self.dmark("d_agz", inst, "agz")
        for dt_ in range(NDT):
            inst = self.q.dma_start(t["xT"][:, dt_ * T:(dt_ + 1) * T],
                                    t["d_x"][dt_ * 128:(dt_ + 1) * 128, :]) if self.on("sy") else None
            self.dmark("d_x%d" % dt_, inst, ("xin", dt_))

        cpairs = [
            (t["onescol_bf"][:, :], t["d_onesc"][:, :]),
        ]
        wg_src = bass.AP(t["d_wg"], 0, [[8, 128], [1024, 8], [1, 8]])
        cpairs.append((t["wg_bf"][:, :].rearrange("p (dc e) -> p dc e", e=8), wg_src))
        for dst, src in cpairs:
            inst = self.q.dma_start(dst, src) if self.on("gp") else None
            self.dmark("d_wc", inst)
        self.M["wc_all"] = ("d_wc", self.d["d_wc"])

        self.ring_refill("w1", 0)
        self.ring_refill("w2", 0)
        self.ring_refill("w1", 1)
        self.ring_refill("w2", 1)
        self.w3_stream(0, 0, wait_key=None)

        if self.on("gp"):
            self.wt("gp", "agz")
            self.q.collective_compute(
                "AllGather", ALU.bypass,
                replica_groups=[[0, 1, 2, 3, 4, 5, 6, 7]],
                ins=[t["d_agin"].ap().opt()],
                outs=[t["d_agout"].ap().opt()],
            ).then_inc(self.S["s_cc"])

        self.wt("sc", "in_all")
        inst = self.q.activation(t["ah_s"][:, :], t["av_s"][:, :], AF.Tanh) if self.on("sc") else None
        self.mark("sc", inst, "ah")

        self.wt("ve", "ah")
        if self.on("ve"):
            self.q.tensor_copy(t["abig_s"][:, :], t["ah_s"][:, :])
            self.q.drain()
            for _ in range(11):
                self.q.tensor_tensor(t["abig_s"][:, :], t["abig_s"][:, :], t["abig_s"][:, :], ALU.mult)
                self.q.drain()
            inst = self.q.tensor_tensor(t["a2_s"][:, :], t["abig_s"][:, :], t["abig_s"][:, :], ALU.mult)
        else:
            inst = None
        self.mark("ve", inst, "apow")
        self.dr("ve")

        if self.on("ve"):
            v = self.q
            for j in range(8):
                w8j = t["w8_s"][:, j * 8:(j + 1) * 8]
                v.tensor_scalar(w8j, t["a2_s"][:, :], t["gm_s"][:, 16 + j:17 + j], None, ALU.mult)
                v.drain()
                v.scalar_tensor_tensor(w8j, t["abig_s"][:, :], t["gm_s"][:, 8 + j:9 + j],
                                       w8j, ALU.mult, ALU.add)
                v.drain()
                inst = v.tensor_scalar(w8j, w8j, t["gm_s"][:, j:j + 1], None, ALU.add)
                v.drain()
        else:
            inst = None
        self.mark("ve", inst, "w8")

    def p_staging(self):
        t = self.t
        zt = t["hid"][:, 7 * T:8 * T]
        bufs = [t["tmpb"][:, :], t["gbc"][:, 0:T], t["gbc"][:, T:2 * T]]
        if self.on("ve"):
            self.q.memset(zt, 0.0)
            self.q.drain()
        for dt_ in range(NDT):
            buf = bufs[dt_ % 3]
            self.wt("ve", ("pout", dt_ - 3), opt=True)
            if self.on("ve"):
                dec = bass.AP(t["ah_s"], dt_, [[8, 128], [0, T]])
                isc = self.q.tensor_tensor_scan(buf, dec, zt, t["onescf"][:, 0:1], ALU.mult, ALU.add)
                self.sep(isc)
            else:
                isc = None
            self.mark("ve", isc, ("pst", dt_))
            self.wt("sy", ("pst", dt_))
            inst = self.q.dma_start(t["d_P"][:, dt_ * T:(dt_ + 1) * T], buf) if self.on("sy") else None
            self.dmark("d_po%d" % (dt_ % 3), inst, ("pout", dt_))

    def ring_refill(self, which, spi):
        t = self.t
        phys = spi % 2
        g = spi
        pair = g % 8
        g //= 8
        e = g % E
        p = g // E
        dram = t["d_w1"] if which == "w1" else t["d_w2"]
        ring = t["w1r"] if which == "w1" else t["w2r"]
        prev = spi - 2
        if prev >= 0:
            self.wt("gp", (which + "_used", prev))
        src = bass.AP(dram, e * D * FF + pair * 256, [[FF, 128], [128 * FF, 8], [1, 256]])
        dst = ring[:, phys * 2048:(phys + 1) * 2048].rearrange("p (dc f) -> p dc f", f=256)
        inst = self.q.dma_start(dst, src) if self.on("gp") else None
        self.dmark(which + "_" + str(phys), inst, (which + "slot", spi))

    def w3_stream(self, p, e, wait_key):
        t = self.t
        if wait_key is not None:
            self.wt("gp", wait_key)
        for fc2 in range(NFT // 2):
            src = bass.AP(t["d_w3"], (e * FF + fc2 * 256) * D,
                          [[D, 128], [128 * D, 2], [1, D]])
            dst = t["w3b"][:, fc2 * 2048:(fc2 + 1) * 2048].rearrange(
                "p (s d) -> p s d", d=D)
            inst = self.q.dma_start(dst, src) if self.on("gp") else None
            self.dmark("d_w3", inst)
        self.M[("w3", p, e)] = ("d_w3", self.d["d_w3"])

    def wout_stream(self):
        t = self.t
        self.wt("gp", ("bmm", 0, E - 1, 1, NDT - 1))
        for dc in range(NDT):
            src = t["d_wo"][dc * 128:(dc + 1) * 128, :]
            dst = t["w3b"][:, dc * 1024:(dc + 1) * 1024]
            inst = self.q.dma_start(dst, src) if self.on("gp") else None
            self.dmark("d_wo", inst)
        self.M["wo"] = ("d_wo", self.d["d_wo"])

    def layernorm(self, tag, waits_sc_pe=(), waits_sc_ve=(), conv_dt_waits=None,
                  xsq_ve_waits=(), stats_pe_waits=()):
        t = self.t
        if self.on("sc"):
            for (en, key) in waits_sc_pe:
                self.wt("sc", key)
            for (en, key) in waits_sc_ve:
                self.wt("sc", key)
            self.wt("sc", "in_all")
        for dt_ in range(NDT):
            if self.on("sc"):
                self.wt("sc", ("xin", dt_))
                if conv_dt_waits is not None:
                    for key in conv_dt_waits(dt_):
                        self.wt("sc", key)
            inst = self.q.activation(t["xn"][:, dt_ * T:(dt_ + 1) * T],
                                     t["xT"][:, dt_ * T:(dt_ + 1) * T], AF.Copy) if self.on("sc") else None
            self.mark("sc", inst, ("xbf", tag, dt_))
        if self.on("ve"):
            for key in xsq_ve_waits:
                self.wt("ve", key)
        for dt_ in range(NDT):
            self.wt("ve", ("xin", dt_))
            inst = self.q.tensor_tensor(t["hid"][:, dt_ * T:(dt_ + 1) * T],
                                        t["xT"][:, dt_ * T:(dt_ + 1) * T],
                                        t["xT"][:, dt_ * T:(dt_ + 1) * T], ALU.mult) if self.on("ve") else None
            self.mark("ve", inst, ("xsq", tag, dt_))

        self.wt("pe", "wc_all")
        banks = [self.t["ph1"], self.t["ph2"], self.t["pxA"], self.t["pxB"]]
        def sreg(kind, tg):
            b = banks[tg // 2] if kind == 0 else banks[2 + tg // 2]
            return b[0:1, (tg % 2) * 512:(tg % 2) * 512 + 512]
        for key in stats_pe_waits:
            self.wt("pe", key)
        for dt_ in range(NDT):
            self.wt("pe", ("xbf", tag, dt_))
            for tg in range(4):
                inst = self.q.matmul(sreg(0, tg), t["onescol_bf"][:, :],
                                     t["xn"][:, dt_ * T + tg * 512: dt_ * T + tg * 512 + 512],
                                     start=(dt_ == 0), stop=(dt_ == NDT - 1)) if self.on("pe") else None
        for dt_ in range(NDT):
            self.wt("pe", ("xsq", tag, dt_))
            for tg in range(4):
                inst = self.q.matmul(sreg(1, tg), t["onescol_bf"][:, :],
                                     t["hid"][:, dt_ * T + tg * 512: dt_ * T + tg * 512 + 512],
                                     start=(dt_ == 0), stop=(dt_ == NDT - 1)) if self.on("pe") else None
        self.mark("pe", inst, ("stmm", tag))
        self.wt("ve", ("stmm", tag))
        if self.on("ve"):
            for tg in range(4):
                self.q.tensor_copy(t["sums_r"][:, tg * 512:(tg + 1) * 512], sreg(0, tg))
            for tg in range(4):
                inst = self.q.tensor_copy(t["sumsq_r"][:, tg * 512:(tg + 1) * 512], sreg(1, tg))
        else:
            inst = None
        self.mark("ve", inst, ("stev", tag))
        self.dr("ve")

        row3 = t["pxB"][0:1, 0:1024]
        for h in range(2):
            mh = t["sums_r"][:, h * 1024:(h + 1) * 1024]
            qh = t["sumsq_r"][:, h * 1024:(h + 1) * 1024]
            if self.on("ve"):
                if h == 0:
                    self.q.tensor_scalar(t["sums_r"][:, :], t["sums_r"][:, :], 1.0 / D, None, ALU.mult)
                    self.q.drain()
                    self.q.tensor_scalar(t["sumsq_r"][:, :], t["sumsq_r"][:, :], 1.0 / D, None, ALU.mult)
                    self.q.drain()
                self.wt("ve", ("sd", tag, h - 1), opt=True)
                self.q.tensor_tensor(row3, mh, mh, ALU.mult)
                self.q.drain()
                inst = self.q.tensor_tensor(qh, qh, row3, ALU.subtract)
            else:
                inst = None
            self.mark("ve", inst, ("var", tag, h))
            self.dr("ve")
            self.wt("sc", ("var", tag, h))
            inst = self.q.activation(row3, qh, AF.Sqrt, bias=t["eps_s"][0:1, 0:1]) if self.on("sc") else None
            self.mark("sc", inst, ("sd", tag, h))
            self.wt("ve", ("sd", tag, h))
            inst = self.q.reciprocal(qh, row3) if self.on("ve") else None
            self.mark("ve", inst, ("rr", tag, h))
            self.dr("ve")
        self.M[("rr", tag)] = self.M[("rr", tag, 1)]

        mb, rb = t["mbf"], t["rbf"]
        pxA = t["pxA"]
        self.wt("pe", ("stev", tag))
        for (ri, (row, dst)) in enumerate(((t["sums_r"], mb), (t["sumsq_r"], rb))):
            for tg in range(4):
                bank = pxA[:, 0:512] if ((ri * 4 + tg) % 2 == 0) else pxA[:, 512:1024]
                if self.on("pe"):
                    self.wt("pe", ("rr", tag, tg // 2))
                    if ri * 4 + tg >= 2:
                        self.wt("pe", ("brev", tag, ri * 4 + tg - 2))
                inst = self.q.matmul(bank, t["onesr_s"][:, :], row[:, tg * 512:(tg + 1) * 512],
                                     start=True, stop=True) if self.on("pe") else None
                self.mark("pe", inst, ("brmm", tag, ri * 4 + tg))
                self.wt("ve", ("brmm", tag, ri * 4 + tg))
                inst = self.q.tensor_copy(dst[:, tg * 512:(tg + 1) * 512], bank) if self.on("ve") else None
                self.mark("ve", inst, ("brev", tag, ri * 4 + tg))
        self.M[("mbrb", tag)] = self.M[("brev", tag, 7)]

    def ffn_pass(self, p):
        t = self.t
        tag = "f%d" % p
        if p == 0:
            self.layernorm(tag)
        else:
            self.layernorm(tag,
                           waits_sc_pe=[("pe", ("wmm", 7, 3))],
                           waits_sc_ve=[("ve", ("wev", 7, 3))])
            self.wt("pe", ("wev", 7, 3))

        self.wt("ve", ("stmm", tag))
        self.wt("gp", ("mbrb", tag))
        for dt_ in range(NDT):
            eng = "ve" if dt_ < 5 else "gp"
            if self.on(eng):
                xs = t["xn"][:, dt_ * T:(dt_ + 1) * T]
                i1 = self.q.tensor_tensor(xs, t["xT"][:, dt_ * T:(dt_ + 1) * T], t["mbf"][:, :], ALU.subtract)
                self.sep(i1)
                inst = self.q.tensor_tensor(xs, xs, t["rbf"][:, :], ALU.mult)
            else:
                inst = None
            self.mark(eng, inst, ("xn", p, dt_))

        self.gating(p)
        if p == 0:
            self.p_staging()

        self.gbc_refill(p, 0)
        self.gbc_refill(p, 1)

        for e in range(E):
            if not (p == 0 and e == 0):
                if e > 0:
                    self.w3_stream(p, e, ("bmm", p, e - 1, 1, NDT - 1))
                else:
                    self.w3_stream(p, e, ("wmm", 7, 3))
            if e >= 2:
                self.gbc_refill(p, e)
            for fh in range(2):
                self.a_phase(p, e, fh)
                self.b_phase(p, e, fh)

    def gbc_refill(self, p, e):
        t = self.t
        if e >= 2:
            self.wt("gp", ("bmul", ((p * E + e - 2) * 2 + 1) * NDT + NDT - 1))
        else:
            for q4 in range(4):
                self.wt("gp", ("gfmd%d" % q4, p))
            if p == 0 and e == 0:
                self.wt("gp", ("pout", 7))
            if p == 0 and e == 1:
                self.wt("gp", ("pout", 5))
        ap = bass.AP(t["d_gfm"], e * T, [[0, 128], [1, T]])
        inst = self.q.dma_start(t["gbc"][:, (e % 2) * T:(e % 2 + 1) * T], ap) if self.on("gp") else None
        self.dmark("d_gb%d" % (e % 2), inst, ("gbcd", p, e))

    def a_phase(self, p, e, fh):
        t = self.t
        h1t = (t["ph1"], t["ph2"])
        h2t = (t["pxA"], t["pxB"])
        for ftl in range(8):
            ft = fh * 8 + ftl
            gft = p * 128 + e * 16 + ft
            spi = (p * E + e) * 8 + ft // 2
            sub = ft % 2
            phys = spi % 2
            gb_base = ((p * E + e) * 2 + fh) * NDT
            if self.on("pe"):
                if sub == 0:
                    self.wt("pe", ("w1slot", spi))
                self.wt("pe", ("silu", gft - 1), opt=True)
                if ftl == 0:
                    self.wt("pe", ("bmul", gb_base - 8 + 6), opt=True)
                if e == 0 and fh == 0 and ftl == 0:
                    self.wt("pe", ("gev", p, 3))
                if p == 0 and e == 0 and ft == 0:
                    self.wt("pe", ("xn", p, 4))
                    self.wt("pe", ("xn", p, NDT - 1))
                for dc in range(NDT):
                    lo = phys * 2048 + dc * 256 + sub * 128
                    l = t["w1r"][:, lo: lo + 128]
                    for tg in range(4):
                        rhs = t["xn"][:, dc * T + tg * 512: dc * T + tg * 512 + 512]
                        inst = self.q.matmul(h1t[tg // 2][:, (tg % 2) * 512:(tg % 2) * 512 + 512],
                                             l, rhs, start=(dc == 0), stop=(dc == NDT - 1))
            else:
                inst = None
            self.mark("pe", inst, ("h1", gft))
            if sub == 1:
                self.M[("w1_used", spi)] = self.M[("h1", gft)]
            if self.on("pe"):
                if sub == 0:
                    self.wt("pe", ("w2slot", spi))
                self.wt("pe", ("hid", gft - 1), opt=True)
                if ftl == 0:
                    self.wt("pe", ("bmul", gb_base - 8 + 7), opt=True)
                if e == 0 and fh == 0 and ftl == 0:
                    self.wt("pe", ("gev", p, 3))
                    self.wt("pe", ("mbrb", "f%d" % p))
                for dc in range(NDT):
                    lo = phys * 2048 + dc * 256 + sub * 128
                    l = t["w2r"][:, lo: lo + 128]
                    for tg in range(4):
                        rhs = t["xn"][:, dc * T + tg * 512: dc * T + tg * 512 + 512]
                        inst = self.q.matmul(h2t[tg // 2][:, (tg % 2) * 512:(tg % 2) * 512 + 512],
                                             l, rhs, start=(dc == 0), stop=(dc == NDT - 1))
            else:
                inst = None
            self.mark("pe", inst, ("h2", gft))
            if sub == 1:
                self.M[("w2_used", spi)] = self.M[("h2", gft)]
                if spi + 2 < 2 * E * 8:
                    self.ring_refill("w1", spi + 2)
                    self.ring_refill("w2", spi + 2)
            tpar = (gft % 2) * 2048
            self.wt("sc", ("h1", gft))
            self.wt("sc", ("hid", gft - 2), opt=True)
            inst = self.q.activation(t["tsil"][:, tpar: tpar + 1024], t["ph1"][:, 0:1024], AF.Silu) if self.on("sc") else None
            self.mark("sc", inst, ("siluA", gft))
            inst = self.q.activation(t["tsil"][:, tpar + 1024: tpar + 2048], t["ph2"][:, 0:1024], AF.Silu) if self.on("sc") else None
            self.mark("sc", inst, ("silu", gft))
            self.wt("ve", ("siluA", gft))
            self.wt("ve", ("h2", gft))
            inst = self.q.tensor_tensor(t["hid"][:, ftl * T: ftl * T + 1024],
                                        t["tsil"][:, tpar: tpar + 1024], t["pxA"][:, 0:1024], ALU.mult) if self.on("ve") else None
            self.mark("ve", inst, ("hidA", gft))
            self.wt("ve", ("silu", gft))
            inst = self.q.tensor_tensor(t["hid"][:, ftl * T + 1024: (ftl + 1) * T],
                                        t["tsil"][:, tpar + 1024: tpar + 2048], t["pxB"][:, 0:1024], ALU.mult) if self.on("ve") else None
            self.mark("ve", inst, ("hid", gft))

    def b_phase(self, p, e, fh):
        t = self.t
        gft_last = p * 128 + e * 16 + fh * 8 + 7
        for dt_ in range(NDT):
            gb = ((p * E + e) * 2 + fh) * NDT + dt_
            pair = (t["ph1"], t["ph2"]) if (dt_ % 2 == 0) else (t["pxA"], t["pxB"])
            if self.on("pe"):
                if dt_ == 0:
                    self.wt("pe", ("w3", p, e))
                    self.wt("pe", ("gbcd", p, e))
                    self.wt("pe", ("silu", gft_last))
                if dt_ == 1:
                    self.wt("pe", ("hid", gft_last))
                self.wt("pe", ("bmul", gb - 2), opt=True)
                if dt_ <= 1 and e == 0 and fh == 0:
                    self.wt("pe", "gfm%d" % p)
                for fcl in range(8):
                    fc = fh * 8 + fcl
                    if dt_ == 0:
                        self.wt("pe", ("hid", p * 128 + e * 16 + fc))
                    l = t["w3b"][:, fc * 1024 + dt_ * 128: fc * 1024 + (dt_ + 1) * 128]
                    for tg in range(4):
                        rhs = t["hid"][:, fcl * T + tg * 512: fcl * T + tg * 512 + 512]
                        inst = self.q.matmul(pair[tg // 2][:, (tg % 2) * 512:(tg % 2) * 512 + 512],
                                             l, rhs, start=(fcl == 0), stop=(fcl == 7))
            else:
                inst = None
            self.mark("pe", inst, ("bmm", p, e, fh, dt_))
            self.wt("ve", ("bmm", p, e, fh, dt_))
            self.wt("ve", ("gbcd", p, e))
            if gb == 0:
                self.wt("ve", ("pout", 6))
            for half in range(2):
                px = pair[half]
                gsl = t["gbc"][:, (e % 2) * T + half * 1024: (e % 2) * T + (half + 1) * 1024]
                tb = t["tmpb"][:, half * 1024:(half + 1) * 1024]
                if self.on("ve"):
                    im = self.q.tensor_tensor(tb, px[:, 0:1024], gsl, ALU.mult)
                    self.q.drain()
                    xs = t["xT"][:, dt_ * T + half * 1024: dt_ * T + (half + 1) * 1024]
                    inst = self.q.tensor_tensor(xs, xs, tb, ALU.add)
                else:
                    im = inst = None
                if half == 1:
                    self.mark("ve", im, ("bmul", gb))
                    self.mark("ve", inst, ("bev", gb))

    def gating(self, p):
        t = self.t
        pxB = t["pxB"]
        sbanks = [t["pxB"][:, 0:8], t["ph1"][:, 0:8], t["ph2"][:, 0:8], t["pxA"][:, 0:8]]
        for tt in range(NFT):
            if self.on("pe"):
                if tt == 0 and p == 1:
                    self.wt("pe", ("wev", 7, 3))
                if tt % 4 == 0 and tt >= 4:
                    self.wt("pe", ("sev4", p, tt // 4 - 1))
                dcs = [0, 5, 1, 6, 2, 7, 3, 4] if tt == 0 else list(range(NDT))
                for di, dc in enumerate(dcs):
                    if tt == 0:
                        self.wt("pe", ("xn", p, dc))
                    l = t["xn"][:, dc * T + tt * 128: dc * T + (tt + 1) * 128]
                    inst = self.q.matmul(sbanks[tt % 4], l, t["wg_bf"][:, dc * 8:(dc + 1) * 8],
                                         start=(di == 0), stop=(di == NDT - 1))
            else:
                inst = None
            self.mark("pe", inst, ("smm", p, tt))
            if tt % 4 == 3:
                self.wt("ve", ("smm", p, tt))
                if self.on("ve"):
                    for q4 in range(4):
                        t4 = tt - 3 + q4
                        inst = self.q.tensor_copy(t["sc_tm"][:, t4 * 8:(t4 + 1) * 8], sbanks[t4 % 4])
                else:
                    inst = None
                self.mark("ve", inst, ("sev4", p, tt // 4))

        if self.on("ve"):
            v = self.q
            v.drain()
            s3 = t["sc_tm"][:, :].rearrange("p (g k) -> p g k", k=8)
            m1b = bass.AP(t["m1"], 0, [[16, 128], [1, 16], [0, 8]])
            m2b = bass.AP(t["m2"], 0, [[16, 128], [1, 16], [0, 8]])
            i1b = bass.AP(t["i1"], 0, [[16, 128], [1, 16], [0, 8]])
            i2b = bass.AP(t["i2"], 0, [[16, 128], [1, 16], [0, 8]])
            p1b = bass.AP(t["p1"], 0, [[16, 128], [1, 16], [0, 8]])
            p2b = bass.AP(t["p2"], 0, [[16, 128], [1, 16], [0, 8]])
            wk0, wk1 = t["wk0"], t["wk1"]
            v.tensor_reduce(t["m1"][:, :], s3, AXX, ALU.max)
            v.drain()
            v.tensor_tensor(wk0[:, :].rearrange("p (g k) -> p g k", k=8), s3, m1b, ALU.is_equal)
            v.drain()
            v.scalar_tensor_tensor(wk1[:, :], wk0[:, :], -64.0, t["iota_s"][:, :], ALU.mult, ALU.add)
            v.drain()
            v.tensor_scalar(wk1[:, :], wk1[:, :], 64.0, None, ALU.add)
            v.drain()
            v.tensor_reduce(t["i1"][:, :], wk1[:, :].rearrange("p (g k) -> p g k", k=8), AXX, ALU.min)
            v.drain()
            v.tensor_tensor(t["mk1"][:, :].rearrange("p (g k) -> p g k", k=8),
                            t["iota_s"][:, :].rearrange("p (g k) -> p g k", k=8), i1b, ALU.is_equal)
            v.drain()
            v.scalar_tensor_tensor(wk0[:, :], t["mk1"][:, :], -1e9, t["sc_tm"][:, :], ALU.mult, ALU.add)
            v.drain()
            v.tensor_reduce(t["m2"][:, :], wk0[:, :].rearrange("p (g k) -> p g k", k=8), AXX, ALU.max)
            v.drain()
            v.tensor_tensor(wk1[:, :].rearrange("p (g k) -> p g k", k=8),
                            wk0[:, :].rearrange("p (g k) -> p g k", k=8), m2b, ALU.is_equal)
            v.drain()
            v.scalar_tensor_tensor(wk0[:, :], wk1[:, :], -64.0, t["iota_s"][:, :], ALU.mult, ALU.add)
            v.drain()
            v.tensor_scalar(wk0[:, :], wk0[:, :], 64.0, None, ALU.add)
            v.drain()
            v.tensor_reduce(t["i2"][:, :], wk0[:, :].rearrange("p (g k) -> p g k", k=8), AXX, ALU.min)
            v.drain()
            v.tensor_tensor(t["mk2"][:, :].rearrange("p (g k) -> p g k", k=8),
                            t["iota_s"][:, :].rearrange("p (g k) -> p g k", k=8), i2b, ALU.is_equal)
            v.drain()
            inst = v.tensor_tensor(t["d12"][:, :], t["m1"][:, :], t["m2"][:, :], ALU.subtract)
        else:
            inst = None
        self.mark("ve", inst, ("d12", p))
        self.dr("ve")
        self.wt("sc", ("d12", p))
        inst = self.q.activation(t["p1"][:, :], t["d12"][:, :], AF.Sigmoid) if self.on("sc") else None
        self.mark("sc", inst, ("sig", p))
        if self.on("ve"):
            v = self.q
            self.wt("ve", ("sig", p))
            v.tensor_scalar(t["p2"][:, :], t["p1"][:, :], -0.5, 0.5, ALU.mult, ALU.add)
            v.drain()
            v.tensor_scalar(t["p1"][:, :], t["p1"][:, :], 0.5, None, ALU.mult)
            v.drain()
            p1b = bass.AP(t["p1"], 0, [[16, 128], [1, 16], [0, 8]])
            p2b = bass.AP(t["p2"], 0, [[16, 128], [1, 16], [0, 8]])
            v.tensor_tensor(t["wk0"][:, :].rearrange("p (g k) -> p g k", k=8),
                            t["mk1"][:, :].rearrange("p (g k) -> p g k", k=8), p1b, ALU.mult)
            v.drain()
            v.tensor_tensor(t["wk1"][:, :].rearrange("p (g k) -> p g k", k=8),
                            t["mk2"][:, :].rearrange("p (g k) -> p g k", k=8), p2b, ALU.mult)
            v.drain()
            inst = v.tensor_tensor(t["g_tm"][:, :], t["wk0"][:, :], t["wk1"][:, :], ALU.add)
        else:
            inst = None
        self.mark("ve", inst, ("gtm", p))
        self.dr("ve")

        self.wt("pe", ("gtm", p))
        self.wt("pe", ("mbrb", "f%d" % p))
        tbanks = [t["pxB"][0:8, 0:128], t["ph1"][0:8, 0:128], t["ph2"][0:8, 0:128], t["pxA"][0:8, 0:128]]
        for tt in range(NFT):
            if self.on("pe"):
                if tt % 4 == 0 and tt >= 4:
                    self.wt("pe", ("gev", p, tt // 4 - 1))
                inst = self.q.transpose(tbanks[tt % 4], t["g_tm"][:, tt * 8:(tt + 1) * 8], t["ident_s"][:, :])
            else:
                inst = None
            self.mark("pe", inst, ("gtr", p, tt))
            if tt % 4 == 3:
                self.wt("ve", ("gtr", p, tt))
                if self.on("ve"):
                    if tt >= 4:
                        for q4 in range(4):
                            self.wt("ve", ("gfdma", p, tt - 7 + q4))
                    for q4 in range(4):
                        t4 = tt - 3 + q4
                        inst = self.q.tensor_copy(t["g_blk"][:, (t4 % 4) * 128:(t4 % 4 + 1) * 128],
                                                  tbanks[t4 % 4])
                else:
                    inst = None
                self.mark("ve", inst, ("gev", p, tt // 4))
                for q4 in range(4):
                    t4 = tt - 3 + q4
                    self.wt("sy", ("gev", p, tt // 4))
                    inst = self.q.dma_start(t["d_gfm"][:, t4 * 128:(t4 + 1) * 128],
                                            t["g_blk"][:, (t4 % 4) * 128:(t4 % 4 + 1) * 128]) if self.on("sy") else None
                    self.dmark("d_gf%d" % (t4 % 4), inst, ("gfdma", p, t4))
        self.M["gfm%d" % p] = self.M[("gev", p, 3)]
        for q4 in range(4):
            self.M[("gfmd%d" % q4, p)] = ("d_gf%d" % q4, self.d["d_gf%d" % q4])

    def attn(self):
        t = self.t
        tag = "at"
        self.layernorm(tag,
                       waits_sc_pe=[("pe", ("h2", 127))],
                       conv_dt_waits=lambda dt_: [("bev", ((0 * E + E - 1) * 2 + 1) * NDT + dt_)],
                       xsq_ve_waits=[("bev", ((0 * E + E - 1) * 2 + 1) * NDT + NDT - 1)],
                       stats_pe_waits=[("bmul", ((0 * E + E - 1) * 2 + 1) * NDT + NDT - 1)])
        hf2, hf3 = t["hf2"], t["hf3"]
        VE_DTS = (0, 1, 2, 3, 4, 5, 6, 7)

        def pslot(dt_):
            if dt_ < 4:
                return t["w3b"][:, 8192 + dt_ * T: 8192 + (dt_ + 1) * T]
            if dt_ < 6:
                return t["gbc"][:, (dt_ - 4) * T:(dt_ - 3) * T]
            if dt_ == 6:
                return t["tsil"][:, 0:T]
            return t["tmpb"][:, :]
        if self.on("gp"):
            self.wt("gp", ("bmm", 0, E - 1, 1, NDT - 1))
            self.wt("gp", ("bev", E * 2 * NDT - 1))
            for dt_ in range(5, 8):
                self.wt("gp", ("pout", dt_))
        pins = [(t["w3b"][:, 8192:16384], 0, 4 * T),
                (t["gbc"][:, 0:2 * T], 4 * T, 2 * T),
                (t["tsil"][:, 0:T], 6 * T, T),
                (t["tmpb"][:, :], 7 * T, T)]
        for i, (dst, off, w) in enumerate(pins):
            inst = self.q.dma_start(dst, t["d_P"][:, off:off + w]) if self.on("gp") else None
            self.dmark("d_pin", inst, ("pin", i))

        for dt_ in VE_DTS:
            if dt_ > 0:
                self.wt("ve", ("yx", dt_ - 1))
            self.bx_into(hf2, dt_, mark_key=("bx1", dt_))
            if self.on("ve"):
                dec = bass.AP(t["ah_s"], dt_, [[8, 128], [0, T]])
                isc = self.q.tensor_tensor_scan(hf3[:, :], dec, hf2[:, :], 0.0, ALU.mult, ALU.add)
                self.sep(isc)
            else:
                isc = None
            self.mark("ve", isc, ("s1m", dt_))
            self.wt("sc", ("s1m", dt_))
            inst = self.q.activation(t["S_all"][:, dt_:dt_ + 1], hf3[:, T - 1:T], AF.Copy) if self.on("sc") else None
            self.mark("sc", inst, ("scol", dt_))
            inst = self.q.activation(t["xn"][:, dt_ * T:(dt_ + 1) * T], hf3[:, :], AF.Copy) if self.on("sc") else None
            self.mark("sc", inst, ("yx", dt_))

        self.wt("sy", ("scol", NDT - 1))
        if self.on("sy"):
            self.q.wait_ge(self.S["s_cc"], 1)
        inst = self.q.dma_start(t["d_agin"][:, :], t["S_all"][:, :]) if self.on("sy") else None
        self.dmark("d_ag1", inst, "ag1")
        self.wout_stream()
        if self.on("gp"):
            self.wt("gp", "ag1")
        if self.on("gp"):
            self.q.collective_compute(
                "AllGather", ALU.bypass,
                replica_groups=[[0, 1, 2, 3, 4, 5, 6, 7]],
                ins=[t["d_agin"].ap().opt()],
                outs=[t["d_agout"].ap().opt()],
            ).then_inc(self.S["s_cc"])
            self.q.wait_ge(self.S["s_cc"], 2)
        sg_src = bass.AP(t["d_agout"], 0, [[8, 128], [1024, 8], [1, 8]])
        inst = self.q.dma_start(t["Sg_all"][:, :].rearrange("p (j k) -> p j k", k=8), sg_src) if self.on("gp") else None
        self.dmark("d_ag2", inst, "ag2")

        if self.on("ve"):
            v = self.q
            self.wt("ve", "ag2")
            v.tensor_tensor(t["Sg_all"][:, :], t["Sg_all"][:, :], t["w8_s"][:, :], ALU.mult)
            v.drain()
            v.tensor_reduce(t["carry"][:, :], bass.AP(t["Sg_all"], 0, [[64, 128], [1, 8], [8, 8]]),
                            AXX, ALU.add)
            v.drain()
            inst = v.engine_nop()
        else:
            inst = None
        self.mark("ve", inst, "carry")

        for dt_ in VE_DTS:
            self.wt("ve", ("yx", dt_))
            if dt_ == 0:
                self.wt("ve", ("pin", 3))
            if self.on("ve"):
                xs = t["xn"][:, dt_ * T:(dt_ + 1) * T]
                inst = self.q.scalar_tensor_tensor(xs, pslot(dt_), t["carry"][:, dt_:dt_ + 1],
                                                   xs, ALU.mult, ALU.add)
            else:
                inst = None
            self.mark("ve", inst, ("yb", dt_))
        self.M["ybve"] = self.M[("yb", NDT - 1)]
        self.M["ybgp"] = self.M["ybve"]

        banks = [(t["pxA"], 0), (t["pxA"], 512), (t["pxB"], 0), (t["pxB"], 512)]
        for b in range(NDT):
            if self.on("pe"):
                if b == 0:
                    self.wt("pe", "wo")
                    self.wt("pe", ("bmul", E * 2 * NDT - 1))
                    self.wt("pe", ("mbrb", tag))
                for dc in range(NDT):
                    self.wt("pe", ("yb", dc))
                    for gi in range(4):
                        g = b * 4 + gi
                        et, tg = g // 4, g % 4
                        px, off = banks[g % 4]
                        if dc == 0 and g >= 4:
                            self.wt("pe", ("wev", (g - 4) // 4, (g - 4) % 4))
                        l = t["w3b"][:, dc * 1024 + et * 128: dc * 1024 + (et + 1) * 128]
                        rhs = t["xn"][:, dc * T + tg * 512: dc * T + tg * 512 + 512]
                        inst = self.q.matmul(px[:, off:off + 512], l, rhs,
                                             start=(dc == 0), stop=(dc == NDT - 1))
                        if dc == NDT - 1:
                            self.mark("pe", inst, ("wmm", et, tg))
            else:
                for gi in range(4):
                    self.mark("pe", None, ("wmm", b, gi))
            for gi in range(4):
                et, tg = b, gi
                px, off = banks[gi]
                self.wt("ve", ("wmm", et, tg))
                if self.on("ve"):
                    xs = t["xT"][:, et * T + tg * 512: et * T + tg * 512 + 512]
                    inst = self.q.scalar_tensor_tensor(xs, px[:, off:off + 512],
                                                       t["bo_s"][:, et:et + 1], xs, ALU.add, ALU.add)
                else:
                    inst = None
                self.mark("ve", inst, ("wev", et, tg))
                self.dr("ve")

    def bx_into(self, dst, dt_, mark_key=None):
        t = self.t
        v = self.q
        if self.on("ve"):
            i1 = v.tensor_tensor(dst[:, :], t["xT"][:, dt_ * T:(dt_ + 1) * T], t["mbf"][:, :], ALU.subtract)
            self.sep(i1)
            i2 = v.scalar_tensor_tensor(dst[:, :], dst[:, :], t["bv_s"][:, dt_:dt_ + 1],
                                        t["rbf"][:, :], ALU.mult, ALU.mult)
        else:
            i2 = None
        if mark_key is not None:
            self.mark("ve", i2, mark_key)
        if self.on("ve"):
            self.sep(i2)

    def output(self):
        t = self.t
        for dt_ in range(NDT):
            self.wt("sy", ("bev", ((E + E - 1) * 2 + 1) * NDT + dt_))
            inst = self.q.dma_start(t["d_out"][dt_ * 128:(dt_ + 1) * 128, :],
                                    t["xT"][:, dt_ * T:(dt_ + 1) * T]) if self.on("sy") else None
            self.dmark("d_out", inst)
        if self.on("sy"):
            self.q.wait_ge(self.S["d_out"], self.d["d_out"])


def build_nc():
    nc = bass.Bass("TRN2", target_bir_lowering=False)
    tn = {}
    d = tn
    d["d_x"] = nc.dram_tensor("x_t", [D, T], F32, kind="ExternalInput")
    d["d_w1"] = nc.dram_tensor("w1t", [E * D, FF], F32, kind="ExternalInput")
    d["d_w2"] = nc.dram_tensor("w2t", [E * D, FF], F32, kind="ExternalInput")
    d["d_w3"] = nc.dram_tensor("w3t", [E * FF, D], F32, kind="ExternalInput")
    d["d_wg"] = nc.dram_tensor("wgt", [D, E], F32, kind="ExternalInput")
    d["d_wo"] = nc.dram_tensor("wot", [D, D], F32, kind="ExternalInput")
    d["d_av"] = nc.dram_tensor("av", [128, 8], F32, kind="ExternalInput")
    d["d_bv"] = nc.dram_tensor("bv", [128, 8], F32, kind="ExternalInput")
    d["d_bo"] = nc.dram_tensor("bo", [128, 8], F32, kind="ExternalInput")
    d["d_ident"] = nc.dram_tensor("identc", [128, 128], F32, kind="ExternalInput")
    d["d_iota"] = nc.dram_tensor("iotac", [128, 128], F32, kind="ExternalInput")
    d["d_gmask"] = nc.dram_tensor("gmaskc", [128, 24], F32, kind="ExternalInput")
    d["d_onesr"] = nc.dram_tensor("onesrc", [1, 128], F32, kind="ExternalInput")
    d["d_onesc"] = nc.dram_tensor("onescc", [128, 1], F32, kind="ExternalInput")
    d["d_eps"] = nc.dram_tensor("epsc", [128, 1], F32, kind="ExternalInput")
    d["d_out"] = nc.dram_tensor("out_t", [D, T], F32, kind="ExternalOutput")
    d["d_agin"] = nc.dram_tensor("ag_in", [1, 1024], F32)
    d["d_gfm"] = nc.dram_tensor("g_fm_d", [8, 2048], F32)
    d["d_agout"] = nc.dram_tensor("ag_out", [8, 1024], F32)
    d["d_P"] = nc.dram_tensor("p_pow_d", [128, NDT * T], BF16)

    A = nc.alloc_sbuf_tensor
    d["xT"] = A("xT", [128, NDT * T], F32)
    d["xn"] = A("xn", [128, NDT * T], BF16)
    d["hid"] = A("hid", [128, 16384], BF16)
    d["w3b"] = A("w3b", [128, 16384], BF16)
    d["w1r"] = A("w1r", [128, 4096], BF16)
    d["w2r"] = A("w2r", [128, 4096], BF16)
    d["tsil"] = A("tsil", [128, 4096], BF16)
    d["gbc"] = A("gbc", [128, 4096], BF16)
    d["tmpb"] = A("tmpb", [128, 2048], BF16)
    d["sc_tm"] = A("sc_tm", [128, 128], F32)
    d["g_tm"] = d["sc_tm"]
    d["g_blk"] = A("g_blk", [8, 512], F32)
    d["wk0"] = A("wk0", [128, 128], F32)
    d["wk1"] = A("wk1", [128, 128], F32)
    d["mk1"] = A("mk1", [128, 128], F32)
    d["mk2"] = A("mk2", [128, 128], F32)
    d["ident_s"] = A("ident_s", [128, 128], F32)
    d["iota_s"] = A("iota_s", [128, 128], F32)
    d["gm_s"] = A("gm_s", [128, 24], F32)
    d["onesr_s"] = A("onesr_s", [1, 128], F32)
    d["eps_s"] = A("eps_s", [128, 1], F32)
    d["onescol_bf"] = A("onescol_bf", [128, 1], BF16)
    d["wg_bf"] = A("wg_bf", [128, 64], BF16)
    d["av_s"] = A("av_s", [128, 8], F32)
    d["bv_s"] = A("bv_s", [128, 8], F32)
    d["bo_s"] = A("bo_s", [128, 8], F32)
    d["ah_s"] = A("ah_s", [128, 8], F32)
    d["abig_s"] = A("abig_s", [128, 8], F32)
    d["a2_s"] = A("a2_s", [128, 8], F32)
    d["S_all"] = A("S_all", [128, 8], F32)
    d["Sg_all"] = A("Sg_all", [128, 64], F32)
    d["carry"] = A("carry", [128, 8], F32)
    d["wc8"] = A("wc8", [128, 8], F32)
    d["w8_s"] = A("w8_s", [128, 64], F32)
    d["onescf"] = A("onescf", [128, 1], F32)
    d["m1"] = A("m1", [128, 16], F32)
    d["m2"] = A("m2", [128, 16], F32)
    d["i1"] = A("i1", [128, 16], F32)
    d["i2"] = A("i2", [128, 16], F32)
    d["p1"] = A("p1", [128, 16], F32)
    d["p2"] = A("p2", [128, 16], F32)
    d["d12"] = A("d12", [128, 16], F32)

    d["ph1"] = nc.alloc_psum_tensor("ph1", [128, 1024], F32)
    d["ph2"] = nc.alloc_psum_tensor("ph2", [128, 1024], F32)
    d["pxA"] = nc.alloc_psum_tensor("pxA", [128, 1024], F32)
    d["pxB"] = nc.alloc_psum_tensor("pxB", [128, 1024], F32)

    hidf = d["hid"].ap().bitcast(F32)
    d["mbf"] = hidf[:, 0 * T:1 * T]
    d["rbf"] = hidf[:, 1 * T:2 * T]
    d["hf2"] = hidf[:, 2 * T:3 * T]
    d["hf3"] = hidf[:, 3 * T:4 * T]
    xnf = d["xn"].ap().bitcast(F32)
    d["sums_r"] = xnf[0:1, 0:T]
    d["sumsq_r"] = xnf[0:1, T:2 * T]

    sems = {}
    names = (["s_pe", "s_ve", "s_sc", "s_gp", "s_sy", "s_cc"] +
             ["d_in", "d_wc", "d_w3", "d_wo", "d_ag1", "d_ag2", "d_out"] +
             ["w1_0", "w1_1", "w2_0", "w2_1", "d_gb0", "d_gb1", "d_gf0", "d_gf1", "d_gf2", "d_gf3", "s_gpc", "d_agz"] + ["d_po%d" % i for i in range(4)] + ["d_pin"] + ["d_x%d" % i for i in range(8)])
    for nm in names:
        sems[nm] = nc.alloc_semaphore(nm)

    with nc.Block() as block:

        @block.tensor
        def _(q):
            Walk("pe", q, nc, tn, sems).run()

        @block.vector
        def _(q):
            Walk("ve", q, nc, tn, sems).run()

        @block.scalar
        def _(q):
            Walk("sc", q, nc, tn, sems).run()

        @block.gpsimd
        def _(q):
            Walk("gp", q, nc, tn, sems).run()

        @block.sync
        def _(q):
            Walk("sy", q, nc, tn, sems).run()

    return nc


_NC_CACHE = None


def _get_nc():
    global _NC_CACHE
    if _NC_CACHE is None:
        _NC_CACHE = build_nc()
    return _NC_CACHE


def kernel(L0_attn_x, L0_attn_a, L0_attn_b, L0_attn_w_out, L0_attn_b_out,
           L0_ffn_Wg, L0_ffn_W1, L0_ffn_W2, L0_ffn_W3):
    x = np.asarray(L0_attn_x, dtype=np.float32)
    a = np.asarray(L0_attn_a, dtype=np.float32)
    b = np.asarray(L0_attn_b, dtype=np.float32)
    w_out = np.asarray(L0_attn_w_out, dtype=np.float32)
    b_out = np.asarray(L0_attn_b_out, dtype=np.float32)
    Wg = np.asarray(L0_ffn_Wg, dtype=np.float32)
    W1 = np.asarray(L0_ffn_W1, dtype=np.float32)
    W2 = np.asarray(L0_ffn_W2, dtype=np.float32)
    W3 = np.asarray(L0_ffn_W3, dtype=np.float32)

    w1t = np.ascontiguousarray(W1.transpose(0, 2, 1)).reshape(E * D, FF)
    w2t = np.ascontiguousarray(W2.transpose(0, 2, 1)).reshape(E * D, FF)
    w3t = np.ascontiguousarray(W3.transpose(0, 2, 1)).reshape(E * FF, D)
    wgt = np.ascontiguousarray(Wg.T)
    wot = np.ascontiguousarray(w_out.T)
    av = np.ascontiguousarray(a.reshape(8, 128).T)
    bv = np.ascontiguousarray(b.reshape(8, 128).T)
    bo = np.ascontiguousarray(b_out.reshape(8, 128).T)

    identc = np.eye(128, dtype=np.float32)
    iotac = np.ascontiguousarray(
        np.broadcast_to(np.tile(np.arange(8, dtype=np.float32), 16)[None, :], (128, 128)))
    onesrc = np.ones((1, 128), dtype=np.float32)
    onescc = np.ones((128, 1), dtype=np.float32)
    epsc = np.full((128, 1), EPS, dtype=np.float32)

    xs = x.reshape(2, 4, T, D)
    in_maps = []
    for c in range(8):
        bb, kk = c // 4, c % 4
        gm = np.zeros((128, 24), dtype=np.float32)
        for ppow in range(3):
            j = kk - 1 - ppow
            if j >= 0:
                gm[:, ppow * 8 + (4 * bb + j)] = 1.0
        in_maps.append({
            "x_t": np.ascontiguousarray(xs[bb, kk].T),
            "w1t": w1t, "w2t": w2t, "w3t": w3t, "wgt": wgt, "wot": wot,
            "av": av, "bv": bv, "bo": bo,
            "identc": identc, "iotac": iotac,
            "gmaskc": gm, "onesrc": onesrc, "onescc": onescc,
            "epsc": epsc,
        })

    nc = _get_nc()
    res = run_bass_kernel_spmd(nc, in_maps, core_ids=list(range(8)))
    out = np.empty((2, 8192, D), dtype=np.float32)
    ov = out.reshape(2, 4, T, D)
    for c in range(8):
        ov[c // 4, c % 4] = res.results[c]["out_t"].T
    return out


# version 33 (final)
# speedup vs baseline: 1.0017x; 1.0017x over previous
import numpy as np
import concourse.bass_utils as _bu
_orig_run_command = _bu.run_command
def _ldwopt_run_command(argv, **kwargs):
    argv = [a.replace("--enable-ldw-opt=false", "--enable-ldw-opt=true")
            if isinstance(a, str) else a for a in argv]
    return _orig_run_command(argv, **kwargs)
_bu.run_command = _ldwopt_run_command
import concourse.bass as bass
import concourse.mybir as mybir
from concourse.bass_utils import run_bass_kernel_spmd

F32 = mybir.dt.float32
BF16 = mybir.dt.bfloat16
AF = mybir.ActivationFunctionType
ALU = mybir.AluOpType
AXX = mybir.AxisListType.X

D = 1024
T = 2048
E = 8
FF = 2048
NDT = 8
NFT = 16
NTH = 2
EPS = 1e-5

ENGS = ("pe", "ve", "sc", "gp", "sy")


class Walk:

    def __init__(self, eng, q, nc, tn, sems):
        self.eng = eng
        self.q = q
        self.nc = nc
        self.t = tn
        self.S = sems
        self.c = {e: 0 for e in ENGS}
        self.gpc = 0
        self.d = {}
        self.M = {}

    def mark(self, engname, inst, key=None):
        self.c[engname] += 1
        if key is not None:
            self.M[key] = ("s_" + engname, self.c[engname])
        if self.eng == engname and inst is not None:
            inst.then_inc(self.S["s_" + engname], 1)

    def dmark(self, semname, inst, key=None):
        self.d[semname] = self.d.get(semname, 0) + 16
        if key is not None:
            self.M[key] = (semname, self.d[semname])
        if self.eng in ("gp", "sy") and inst is not None:
            inst.then_inc(self.S[semname], 16)

    def wt(self, engname, key, opt=False):
        if key not in self.M:
            if not opt:
                raise KeyError("missing milestone %r" % (key,))
            return
        if self.eng != engname:
            return
        nm, val = self.M[key]
        self.q.wait_ge(self.S[nm], val)

    def dr(self, engname):
        if self.eng == engname:
            self.q.drain()

    def sep(self, inst=None):
        if self.eng == "ve":
            self.q.drain()
        elif self.eng == "gp":
            if inst is not None:
                inst.then_inc(self.S["s_gpc"], 1)
            self.gpc += 1
            self.q.wait_ge(self.S["s_gpc"], self.gpc)

    def on(self, engname):
        return self.eng == engname

    def run(self):
        self.setup()
        self.ffn_pass(0)
        self.attn()
        self.ffn_pass(1)
        self.output()

    def setup(self):
        t = self.t
        pairs = [
            (t["ident_s"][:, :], t["d_ident"][:, :]),
            (t["iota_s"][:, :], t["d_iota"][:, :]),
            (t["gm_s"][:, :], t["d_gmask"][:, :]),
            (t["onesr_s"][:, :], t["d_onesr"][:, :]),
            (t["eps_s"][:, :], t["d_eps"][:, :]),
            (t["av_s"][:, :], t["d_av"][:, :]),
            (t["bv_s"][:, :], t["d_bv"][:, :]),
            (t["bo_s"][:, :], t["d_bo"][:, :]),
            (t["onescf"][:, :], t["d_onesc"][:, :]),
        ]
        for dst, src in pairs:
            inst = self.q.dma_start(dst, src) if self.on("sy") else None
            self.dmark("d_in", inst)
        self.M["in_all"] = ("d_in", self.d["d_in"])
        agz_src = bass.AP(t["d_gmask"], 0, [[0, 1], [1, 1024]])
        inst = self.q.dma_start(t["d_wuin"][:, :], agz_src) if self.on("sy") else None
        self.dmark("d_agz", inst, "agz")
        for dt_ in range(NDT):
            inst = self.q.dma_start(t["xT"][:, dt_ * T:(dt_ + 1) * T],
                                    t["d_x"][dt_ * 128:(dt_ + 1) * 128, :]) if self.on("sy") else None
            self.dmark("d_x%d" % dt_, inst, ("xin", dt_))

        cpairs = [
            (t["onescol_bf"][:, :], t["d_onesc"][:, :]),
        ]
        wg_src = bass.AP(t["d_wg"], 0, [[8, 128], [1024, 8], [1, 8]])
        cpairs.append((t["wg_bf"][:, :].rearrange("p (dc e) -> p dc e", e=8), wg_src))
        for dst, src in cpairs:
            inst = self.q.dma_start(dst, src) if self.on("gp") else None
            self.dmark("d_wc", inst)
        self.M["wc_all"] = ("d_wc", self.d["d_wc"])

        self.ring_refill("w1", 0)
        self.ring_refill("w2", 0)
        self.ring_refill("w1", 1)
        self.ring_refill("w2", 1)
        self.w3_stream(0, 0, wait_key=None)

        if self.on("gp"):
            self.wt("gp", "agz")
            self.q.collective_compute(
                "AllGather", ALU.bypass,
                replica_groups=[[0, 1, 2, 3, 4, 5, 6, 7]],
                ins=[t["d_wuin"].ap().opt()],
                outs=[t["d_wuout"].ap().opt()],
            ).then_inc(self.S["s_cc"])

        self.wt("sc", "in_all")
        inst = self.q.activation(t["ah_s"][:, :], t["av_s"][:, :], AF.Tanh) if self.on("sc") else None
        self.mark("sc", inst, "ah")

        self.wt("ve", "ah")
        if self.on("ve"):
            self.q.tensor_copy(t["abig_s"][:, :], t["ah_s"][:, :])
            self.q.drain()
            for _ in range(11):
                self.q.tensor_tensor(t["abig_s"][:, :], t["abig_s"][:, :], t["abig_s"][:, :], ALU.mult)
                self.q.drain()
            inst = self.q.tensor_tensor(t["a2_s"][:, :], t["abig_s"][:, :], t["abig_s"][:, :], ALU.mult)
        else:
            inst = None
        self.mark("ve", inst, "apow")
        self.dr("ve")

        if self.on("ve"):
            v = self.q
            for j in range(8):
                w8j = t["w8_s"][:, j * 8:(j + 1) * 8]
                v.tensor_scalar(w8j, t["a2_s"][:, :], t["gm_s"][:, 16 + j:17 + j], None, ALU.mult)
                v.drain()
                v.scalar_tensor_tensor(w8j, t["abig_s"][:, :], t["gm_s"][:, 8 + j:9 + j],
                                       w8j, ALU.mult, ALU.add)
                v.drain()
                inst = v.tensor_scalar(w8j, w8j, t["gm_s"][:, j:j + 1], None, ALU.add)
                v.drain()
        else:
            inst = None
        self.mark("ve", inst, "w8")

    def p_staging(self):
        t = self.t
        zt = t["hid"][:, 7 * T:8 * T]
        bufs = [t["tmpb"][:, :], t["gbc"][:, 0:T], t["gbc"][:, T:2 * T]]
        if self.on("ve"):
            self.q.memset(zt, 0.0)
            self.q.drain()
        for dt_ in range(NDT):
            buf = bufs[dt_ % 3]
            self.wt("ve", ("pout", dt_ - 3), opt=True)
            if self.on("ve"):
                dec = bass.AP(t["ah_s"], dt_, [[8, 128], [0, T]])
                isc = self.q.tensor_tensor_scan(buf, dec, zt, t["onescf"][:, 0:1], ALU.mult, ALU.add)
                self.sep(isc)
            else:
                isc = None
            self.mark("ve", isc, ("pst", dt_))
            self.wt("sy", ("pst", dt_))
            inst = self.q.dma_start(t["d_P"][:, dt_ * T:(dt_ + 1) * T], buf) if self.on("sy") else None
            self.dmark("d_po%d" % (dt_ % 3), inst, ("pout", dt_))

    def ring_refill(self, which, spi):
        t = self.t
        phys = spi % 2
        g = spi
        pair = g % 8
        g //= 8
        e = g % E
        p = g // E
        dram = t["d_w1"] if which == "w1" else t["d_w2"]
        ring = t["w1r"] if which == "w1" else t["w2r"]
        prev = spi - 2
        if prev >= 0:
            self.wt("gp", (which + "_used", prev))
        src = bass.AP(dram, e * D * FF + pair * 256, [[FF, 128], [128 * FF, 8], [1, 256]])
        dst = ring[:, phys * 2048:(phys + 1) * 2048].rearrange("p (dc f) -> p dc f", f=256)
        inst = self.q.dma_start(dst, src) if self.on("gp") else None
        self.dmark(which + "_" + str(phys), inst, (which + "slot", spi))

    def w3_stream(self, p, e, wait_key):
        t = self.t
        if wait_key is not None:
            self.wt("gp", wait_key)
        for fc2 in range(NFT // 2):
            src = bass.AP(t["d_w3"], (e * FF + fc2 * 256) * D,
                          [[D, 128], [128 * D, 2], [1, D]])
            dst = t["w3b"][:, fc2 * 2048:(fc2 + 1) * 2048].rearrange(
                "p (s d) -> p s d", d=D)
            inst = self.q.dma_start(dst, src) if self.on("gp") else None
            self.dmark("d_w3", inst)
        self.M[("w3", p, e)] = ("d_w3", self.d["d_w3"])

    def wout_stream(self):
        t = self.t
        self.wt("gp", ("bmm", 0, E - 1, 1, NDT - 1))
        for dc in range(NDT):
            src = t["d_wo"][dc * 128:(dc + 1) * 128, :]
            dst = t["w3b"][:, dc * 1024:(dc + 1) * 1024]
            inst = self.q.dma_start(dst, src) if self.on("gp") else None
            self.dmark("d_wo", inst)
        self.M["wo"] = ("d_wo", self.d["d_wo"])

    def layernorm(self, tag, waits_sc_pe=(), waits_sc_ve=(), conv_dt_waits=None,
                  xsq_ve_waits=(), stats_pe_waits=()):
        t = self.t
        if self.on("sc"):
            for (en, key) in waits_sc_pe:
                self.wt("sc", key)
            for (en, key) in waits_sc_ve:
                self.wt("sc", key)
            self.wt("sc", "in_all")
        for dt_ in range(NDT):
            if self.on("sc"):
                self.wt("sc", ("xin", dt_))
                if conv_dt_waits is not None:
                    for key in conv_dt_waits(dt_):
                        self.wt("sc", key)
            inst = self.q.activation(t["xn"][:, dt_ * T:(dt_ + 1) * T],
                                     t["xT"][:, dt_ * T:(dt_ + 1) * T], AF.Copy) if self.on("sc") else None
            self.mark("sc", inst, ("xbf", tag, dt_))
        if self.on("ve"):
            for key in xsq_ve_waits:
                self.wt("ve", key)
        for dt_ in range(NDT):
            self.wt("ve", ("xin", dt_))
            inst = self.q.tensor_tensor(t["hid"][:, dt_ * T:(dt_ + 1) * T],
                                        t["xT"][:, dt_ * T:(dt_ + 1) * T],
                                        t["xT"][:, dt_ * T:(dt_ + 1) * T], ALU.mult) if self.on("ve") else None
            self.mark("ve", inst, ("xsq", tag, dt_))

        self.wt("pe", "wc_all")
        banks = [self.t["ph1"], self.t["ph2"], self.t["pxA"], self.t["pxB"]]
        def sreg(kind, tg):
            b = banks[tg // 2] if kind == 0 else banks[2 + tg // 2]
            return b[0:1, (tg % 2) * 512:(tg % 2) * 512 + 512]
        for key in stats_pe_waits:
            self.wt("pe", key)
        for dt_ in range(NDT):
            self.wt("pe", ("xbf", tag, dt_))
            for tg in range(4):
                inst = self.q.matmul(sreg(0, tg), t["onescol_bf"][:, :],
                                     t["xn"][:, dt_ * T + tg * 512: dt_ * T + tg * 512 + 512],
                                     start=(dt_ == 0), stop=(dt_ == NDT - 1)) if self.on("pe") else None
        for dt_ in range(NDT):
            self.wt("pe", ("xsq", tag, dt_))
            for tg in range(4):
                inst = self.q.matmul(sreg(1, tg), t["onescol_bf"][:, :],
                                     t["hid"][:, dt_ * T + tg * 512: dt_ * T + tg * 512 + 512],
                                     start=(dt_ == 0), stop=(dt_ == NDT - 1)) if self.on("pe") else None
        self.mark("pe", inst, ("stmm", tag))
        self.wt("ve", ("stmm", tag))
        if self.on("ve"):
            for tg in range(4):
                self.q.tensor_copy(t["sums_r"][:, tg * 512:(tg + 1) * 512], sreg(0, tg))
            for tg in range(4):
                inst = self.q.tensor_copy(t["sumsq_r"][:, tg * 512:(tg + 1) * 512], sreg(1, tg))
        else:
            inst = None
        self.mark("ve", inst, ("stev", tag))
        self.dr("ve")

        row3 = t["pxB"][0:1, 0:1024]
        for h in range(2):
            mh = t["sums_r"][:, h * 1024:(h + 1) * 1024]
            qh = t["sumsq_r"][:, h * 1024:(h + 1) * 1024]
            if self.on("ve"):
                if h == 0:
                    self.q.tensor_scalar(t["sums_r"][:, :], t["sums_r"][:, :], 1.0 / D, None, ALU.mult)
                    self.q.drain()
                    self.q.tensor_scalar(t["sumsq_r"][:, :], t["sumsq_r"][:, :], 1.0 / D, None, ALU.mult)
                    self.q.drain()
                self.wt("ve", ("sd", tag, h - 1), opt=True)
                self.q.tensor_tensor(row3, mh, mh, ALU.mult)
                self.q.drain()
                inst = self.q.tensor_tensor(qh, qh, row3, ALU.subtract)
            else:
                inst = None
            self.mark("ve", inst, ("var", tag, h))
            self.dr("ve")
            self.wt("sc", ("var", tag, h))
            inst = self.q.activation(row3, qh, AF.Sqrt, bias=t["eps_s"][0:1, 0:1]) if self.on("sc") else None
            self.mark("sc", inst, ("sd", tag, h))
            self.wt("ve", ("sd", tag, h))
            inst = self.q.reciprocal(qh, row3) if self.on("ve") else None
            self.mark("ve", inst, ("rr", tag, h))
            self.dr("ve")
        self.M[("rr", tag)] = self.M[("rr", tag, 1)]

        mb, rb = t["mbf"], t["rbf"]
        pxA = t["pxA"]
        self.wt("pe", ("stev", tag))
        for (ri, (row, dst)) in enumerate(((t["sums_r"], mb), (t["sumsq_r"], rb))):
            for tg in range(4):
                bank = pxA[:, 0:512] if ((ri * 4 + tg) % 2 == 0) else pxA[:, 512:1024]
                if self.on("pe"):
                    self.wt("pe", ("rr", tag, tg // 2))
                    if ri * 4 + tg >= 2:
                        self.wt("pe", ("brev", tag, ri * 4 + tg - 2))
                inst = self.q.matmul(bank, t["onesr_s"][:, :], row[:, tg * 512:(tg + 1) * 512],
                                     start=True, stop=True) if self.on("pe") else None
                self.mark("pe", inst, ("brmm", tag, ri * 4 + tg))
                self.wt("ve", ("brmm", tag, ri * 4 + tg))
                inst = self.q.tensor_copy(dst[:, tg * 512:(tg + 1) * 512], bank) if self.on("ve") else None
                self.mark("ve", inst, ("brev", tag, ri * 4 + tg))
        self.M[("mbrb", tag)] = self.M[("brev", tag, 7)]

    def ffn_pass(self, p):
        t = self.t
        tag = "f%d" % p
        if p == 0:
            self.layernorm(tag)
        else:
            self.layernorm(tag,
                           waits_sc_pe=[("pe", ("wmm", 7, 3))],
                           waits_sc_ve=[("ve", ("wev", 7, 3))])
            self.wt("pe", ("wev", 7, 3))

        self.wt("ve", ("stmm", tag))
        self.wt("gp", ("mbrb", tag))
        for dt_ in range(NDT):
            eng = "ve" if dt_ < 5 else "gp"
            if self.on(eng):
                xs = t["xn"][:, dt_ * T:(dt_ + 1) * T]
                i1 = self.q.tensor_tensor(xs, t["xT"][:, dt_ * T:(dt_ + 1) * T], t["mbf"][:, :], ALU.subtract)
                self.sep(i1)
                inst = self.q.tensor_tensor(xs, xs, t["rbf"][:, :], ALU.mult)
            else:
                inst = None
            self.mark(eng, inst, ("xn", p, dt_))

        self.gating(p)
        if p == 0:
            self.p_staging()

        self.gbc_refill(p, 0)
        self.gbc_refill(p, 1)

        for e in range(E):
            if not (p == 0 and e == 0):
                if e > 0:
                    self.w3_stream(p, e, ("bmm", p, e - 1, 1, NDT - 1))
                else:
                    self.w3_stream(p, e, ("wmm", 7, 3))
            if e >= 2:
                self.gbc_refill(p, e)
            for fh in range(2):
                self.a_phase(p, e, fh)
                self.b_phase(p, e, fh)

    def gbc_refill(self, p, e):
        t = self.t
        if e >= 2:
            self.wt("gp", ("bmul", ((p * E + e - 2) * 2 + 1) * NDT + NDT - 1))
        else:
            for q4 in range(4):
                self.wt("gp", ("gfmd%d" % q4, p))
            if p == 0 and e == 0:
                self.wt("gp", ("pout", 7))
            if p == 0 and e == 1:
                self.wt("gp", ("pout", 5))
        ap = bass.AP(t["d_gfm"], e * T, [[0, 128], [1, T]])
        inst = self.q.dma_start(t["gbc"][:, (e % 2) * T:(e % 2 + 1) * T], ap) if self.on("gp") else None
        self.dmark("d_gb%d" % (e % 2), inst, ("gbcd", p, e))

    def a_phase(self, p, e, fh):
        t = self.t
        h1t = (t["ph1"], t["ph2"])
        h2t = (t["pxA"], t["pxB"])
        for ftl in range(8):
            ft = fh * 8 + ftl
            gft = p * 128 + e * 16 + ft
            spi = (p * E + e) * 8 + ft // 2
            sub = ft % 2
            phys = spi % 2
            gb_base = ((p * E + e) * 2 + fh) * NDT
            if self.on("pe"):
                if sub == 0:
                    self.wt("pe", ("w1slot", spi))
                self.wt("pe", ("silu", gft - 1), opt=True)
                if ftl == 0:
                    self.wt("pe", ("bmul", gb_base - 8 + 6), opt=True)
                if e == 0 and fh == 0 and ftl == 0:
                    self.wt("pe", ("gev", p, 3))
                if p == 0 and e == 0 and ft == 0:
                    self.wt("pe", ("xn", p, 4))
                    self.wt("pe", ("xn", p, NDT - 1))
                for dc in range(NDT):
                    lo = phys * 2048 + dc * 256 + sub * 128
                    l = t["w1r"][:, lo: lo + 128]
                    for tg in range(4):
                        rhs = t["xn"][:, dc * T + tg * 512: dc * T + tg * 512 + 512]
                        inst = self.q.matmul(h1t[tg // 2][:, (tg % 2) * 512:(tg % 2) * 512 + 512],
                                             l, rhs, start=(dc == 0), stop=(dc == NDT - 1))
            else:
                inst = None
            self.mark("pe", inst, ("h1", gft))
            if sub == 1:
                self.M[("w1_used", spi)] = self.M[("h1", gft)]
            if self.on("pe"):
                if sub == 0:
                    self.wt("pe", ("w2slot", spi))
                self.wt("pe", ("hid", gft - 1), opt=True)
                if ftl == 0:
                    self.wt("pe", ("bmul", gb_base - 8 + 7), opt=True)
                if e == 0 and fh == 0 and ftl == 0:
                    self.wt("pe", ("gev", p, 3))
                    self.wt("pe", ("mbrb", "f%d" % p))
                for dc in range(NDT):
                    lo = phys * 2048 + dc * 256 + sub * 128
                    l = t["w2r"][:, lo: lo + 128]
                    for tg in range(4):
                        rhs = t["xn"][:, dc * T + tg * 512: dc * T + tg * 512 + 512]
                        inst = self.q.matmul(h2t[tg // 2][:, (tg % 2) * 512:(tg % 2) * 512 + 512],
                                             l, rhs, start=(dc == 0), stop=(dc == NDT - 1))
            else:
                inst = None
            self.mark("pe", inst, ("h2", gft))
            if sub == 1:
                self.M[("w2_used", spi)] = self.M[("h2", gft)]
                if spi + 2 < 2 * E * 8:
                    self.ring_refill("w1", spi + 2)
                    self.ring_refill("w2", spi + 2)
            tpar = (gft % 2) * 2048
            self.wt("sc", ("h1", gft))
            self.wt("sc", ("hid", gft - 2), opt=True)
            inst = self.q.activation(t["tsil"][:, tpar: tpar + 1024], t["ph1"][:, 0:1024], AF.Silu) if self.on("sc") else None
            self.mark("sc", inst, ("siluA", gft))
            inst = self.q.activation(t["tsil"][:, tpar + 1024: tpar + 2048], t["ph2"][:, 0:1024], AF.Silu) if self.on("sc") else None
            self.mark("sc", inst, ("silu", gft))
            self.wt("ve", ("siluA", gft))
            self.wt("ve", ("h2", gft))
            inst = self.q.tensor_tensor(t["hid"][:, ftl * T: ftl * T + 1024],
                                        t["tsil"][:, tpar: tpar + 1024], t["pxA"][:, 0:1024], ALU.mult) if self.on("ve") else None
            self.mark("ve", inst, ("hidA", gft))
            self.wt("ve", ("silu", gft))
            inst = self.q.tensor_tensor(t["hid"][:, ftl * T + 1024: (ftl + 1) * T],
                                        t["tsil"][:, tpar + 1024: tpar + 2048], t["pxB"][:, 0:1024], ALU.mult) if self.on("ve") else None
            self.mark("ve", inst, ("hid", gft))

    def b_phase(self, p, e, fh):
        t = self.t
        gft_last = p * 128 + e * 16 + fh * 8 + 7
        for dt_ in range(NDT):
            gb = ((p * E + e) * 2 + fh) * NDT + dt_
            pair = (t["ph1"], t["ph2"]) if (dt_ % 2 == 0) else (t["pxA"], t["pxB"])
            if self.on("pe"):
                if dt_ == 0:
                    self.wt("pe", ("w3", p, e))
                    self.wt("pe", ("gbcd", p, e))
                    self.wt("pe", ("silu", gft_last))
                if dt_ == 1:
                    self.wt("pe", ("hid", gft_last))
                self.wt("pe", ("bmul", gb - 2), opt=True)
                if dt_ <= 1 and e == 0 and fh == 0:
                    self.wt("pe", "gfm%d" % p)
                for fcl in range(8):
                    fc = fh * 8 + fcl
                    if dt_ == 0:
                        self.wt("pe", ("hid", p * 128 + e * 16 + fc))
                    l = t["w3b"][:, fc * 1024 + dt_ * 128: fc * 1024 + (dt_ + 1) * 128]
                    for tg in range(4):
                        rhs = t["hid"][:, fcl * T + tg * 512: fcl * T + tg * 512 + 512]
                        inst = self.q.matmul(pair[tg // 2][:, (tg % 2) * 512:(tg % 2) * 512 + 512],
                                             l, rhs, start=(fcl == 0), stop=(fcl == 7))
            else:
                inst = None
            self.mark("pe", inst, ("bmm", p, e, fh, dt_))
            self.wt("ve", ("bmm", p, e, fh, dt_))
            self.wt("ve", ("gbcd", p, e))
            if gb == 0:
                self.wt("ve", ("pout", 6))
            for half in range(2):
                px = pair[half]
                gsl = t["gbc"][:, (e % 2) * T + half * 1024: (e % 2) * T + (half + 1) * 1024]
                tb = t["tmpb"][:, half * 1024:(half + 1) * 1024]
                if self.on("ve"):
                    im = self.q.tensor_tensor(tb, px[:, 0:1024], gsl, ALU.mult)
                    self.q.drain()
                    xs = t["xT"][:, dt_ * T + half * 1024: dt_ * T + (half + 1) * 1024]
                    inst = self.q.tensor_tensor(xs, xs, tb, ALU.add)
                else:
                    im = inst = None
                if half == 1:
                    self.mark("ve", im, ("bmul", gb))
                    self.mark("ve", inst, ("bev", gb))

    def gating(self, p):
        t = self.t
        pxB = t["pxB"]
        sbanks = [t["pxB"][:, 0:8], t["ph1"][:, 0:8], t["ph2"][:, 0:8], t["pxA"][:, 0:8]]
        for tt in range(NFT):
            if self.on("pe"):
                if tt == 0 and p == 1:
                    self.wt("pe", ("wev", 7, 3))
                if tt % 4 == 0 and tt >= 4:
                    self.wt("pe", ("sev4", p, tt // 4 - 1))
                dcs = [0, 5, 1, 6, 2, 7, 3, 4] if tt == 0 else list(range(NDT))
                for di, dc in enumerate(dcs):
                    if tt == 0:
                        self.wt("pe", ("xn", p, dc))
                    l = t["xn"][:, dc * T + tt * 128: dc * T + (tt + 1) * 128]
                    inst = self.q.matmul(sbanks[tt % 4], l, t["wg_bf"][:, dc * 8:(dc + 1) * 8],
                                         start=(di == 0), stop=(di == NDT - 1))
            else:
                inst = None
            self.mark("pe", inst, ("smm", p, tt))
            if tt % 4 == 3:
                self.wt("ve", ("smm", p, tt))
                if self.on("ve"):
                    for q4 in range(4):
                        t4 = tt - 3 + q4
                        inst = self.q.tensor_copy(t["sc_tm"][:, t4 * 8:(t4 + 1) * 8], sbanks[t4 % 4])
                else:
                    inst = None
                self.mark("ve", inst, ("sev4", p, tt // 4))

        if self.on("ve"):
            v = self.q
            v.drain()
            s3 = t["sc_tm"][:, :].rearrange("p (g k) -> p g k", k=8)
            m1b = bass.AP(t["m1"], 0, [[16, 128], [1, 16], [0, 8]])
            m2b = bass.AP(t["m2"], 0, [[16, 128], [1, 16], [0, 8]])
            i1b = bass.AP(t["i1"], 0, [[16, 128], [1, 16], [0, 8]])
            i2b = bass.AP(t["i2"], 0, [[16, 128], [1, 16], [0, 8]])
            p1b = bass.AP(t["p1"], 0, [[16, 128], [1, 16], [0, 8]])
            p2b = bass.AP(t["p2"], 0, [[16, 128], [1, 16], [0, 8]])
            wk0, wk1 = t["wk0"], t["wk1"]
            v.tensor_reduce(t["m1"][:, :], s3, AXX, ALU.max)
            v.drain()
            v.tensor_tensor(wk0[:, :].rearrange("p (g k) -> p g k", k=8), s3, m1b, ALU.is_equal)
            v.drain()
            v.scalar_tensor_tensor(wk1[:, :], wk0[:, :], -64.0, t["iota_s"][:, :], ALU.mult, ALU.add)
            v.drain()
            v.tensor_scalar(wk1[:, :], wk1[:, :], 64.0, None, ALU.add)
            v.drain()
            v.tensor_reduce(t["i1"][:, :], wk1[:, :].rearrange("p (g k) -> p g k", k=8), AXX, ALU.min)
            v.drain()
            v.tensor_tensor(t["mk1"][:, :].rearrange("p (g k) -> p g k", k=8),
                            t["iota_s"][:, :].rearrange("p (g k) -> p g k", k=8), i1b, ALU.is_equal)
            v.drain()
            v.scalar_tensor_tensor(wk0[:, :], t["mk1"][:, :], -1e9, t["sc_tm"][:, :], ALU.mult, ALU.add)
            v.drain()
            v.tensor_reduce(t["m2"][:, :], wk0[:, :].rearrange("p (g k) -> p g k", k=8), AXX, ALU.max)
            v.drain()
            v.tensor_tensor(wk1[:, :].rearrange("p (g k) -> p g k", k=8),
                            wk0[:, :].rearrange("p (g k) -> p g k", k=8), m2b, ALU.is_equal)
            v.drain()
            v.scalar_tensor_tensor(wk0[:, :], wk1[:, :], -64.0, t["iota_s"][:, :], ALU.mult, ALU.add)
            v.drain()
            v.tensor_scalar(wk0[:, :], wk0[:, :], 64.0, None, ALU.add)
            v.drain()
            v.tensor_reduce(t["i2"][:, :], wk0[:, :].rearrange("p (g k) -> p g k", k=8), AXX, ALU.min)
            v.drain()
            v.tensor_tensor(t["mk2"][:, :].rearrange("p (g k) -> p g k", k=8),
                            t["iota_s"][:, :].rearrange("p (g k) -> p g k", k=8), i2b, ALU.is_equal)
            v.drain()
            inst = v.tensor_tensor(t["d12"][:, :], t["m1"][:, :], t["m2"][:, :], ALU.subtract)
        else:
            inst = None
        self.mark("ve", inst, ("d12", p))
        self.dr("ve")
        self.wt("sc", ("d12", p))
        inst = self.q.activation(t["p1"][:, :], t["d12"][:, :], AF.Sigmoid) if self.on("sc") else None
        self.mark("sc", inst, ("sig", p))
        if self.on("ve"):
            v = self.q
            self.wt("ve", ("sig", p))
            v.tensor_scalar(t["p2"][:, :], t["p1"][:, :], -0.5, 0.5, ALU.mult, ALU.add)
            v.drain()
            v.tensor_scalar(t["p1"][:, :], t["p1"][:, :], 0.5, None, ALU.mult)
            v.drain()
            p1b = bass.AP(t["p1"], 0, [[16, 128], [1, 16], [0, 8]])
            p2b = bass.AP(t["p2"], 0, [[16, 128], [1, 16], [0, 8]])
            v.tensor_tensor(t["wk0"][:, :].rearrange("p (g k) -> p g k", k=8),
                            t["mk1"][:, :].rearrange("p (g k) -> p g k", k=8), p1b, ALU.mult)
            v.drain()
            v.tensor_tensor(t["wk1"][:, :].rearrange("p (g k) -> p g k", k=8),
                            t["mk2"][:, :].rearrange("p (g k) -> p g k", k=8), p2b, ALU.mult)
            v.drain()
            inst = v.tensor_tensor(t["g_tm"][:, :], t["wk0"][:, :], t["wk1"][:, :], ALU.add)
        else:
            inst = None
        self.mark("ve", inst, ("gtm", p))
        self.dr("ve")

        self.wt("pe", ("gtm", p))
        self.wt("pe", ("mbrb", "f%d" % p))
        tbanks = [t["pxB"][0:8, 0:128], t["ph1"][0:8, 0:128], t["ph2"][0:8, 0:128], t["pxA"][0:8, 0:128]]
        for tt in range(NFT):
            if self.on("pe"):
                if tt % 4 == 0 and tt >= 4:
                    self.wt("pe", ("gev", p, tt // 4 - 1))
                inst = self.q.transpose(tbanks[tt % 4], t["g_tm"][:, tt * 8:(tt + 1) * 8], t["ident_s"][:, :])
            else:
                inst = None
            self.mark("pe", inst, ("gtr", p, tt))
            if tt % 4 == 3:
                self.wt("ve", ("gtr", p, tt))
                if self.on("ve"):
                    if tt >= 4:
                        for q4 in range(4):
                            self.wt("ve", ("gfdma", p, tt - 7 + q4))
                    for q4 in range(4):
                        t4 = tt - 3 + q4
                        inst = self.q.tensor_copy(t["g_blk"][:, (t4 % 4) * 128:(t4 % 4 + 1) * 128],
                                                  tbanks[t4 % 4])
                else:
                    inst = None
                self.mark("ve", inst, ("gev", p, tt // 4))
                for q4 in range(4):
                    t4 = tt - 3 + q4
                    self.wt("sy", ("gev", p, tt // 4))
                    inst = self.q.dma_start(t["d_gfm"][:, t4 * 128:(t4 + 1) * 128],
                                            t["g_blk"][:, (t4 % 4) * 128:(t4 % 4 + 1) * 128]) if self.on("sy") else None
                    self.dmark("d_gf%d" % (t4 % 4), inst, ("gfdma", p, t4))
        self.M["gfm%d" % p] = self.M[("gev", p, 3)]
        for q4 in range(4):
            self.M[("gfmd%d" % q4, p)] = ("d_gf%d" % q4, self.d["d_gf%d" % q4])

    def attn(self):
        t = self.t
        tag = "at"
        self.layernorm(tag,
                       waits_sc_pe=[("pe", ("h2", 127))],
                       conv_dt_waits=lambda dt_: [("bev", ((0 * E + E - 1) * 2 + 1) * NDT + dt_)],
                       xsq_ve_waits=[("bev", ((0 * E + E - 1) * 2 + 1) * NDT + NDT - 1)],
                       stats_pe_waits=[("bmul", ((0 * E + E - 1) * 2 + 1) * NDT + NDT - 1)])
        hf2, hf3 = t["hf2"], t["hf3"]
        VE_DTS = (0, 1, 2, 3, 4, 5, 6, 7)

        def pslot(dt_):
            if dt_ < 4:
                return t["w3b"][:, 8192 + dt_ * T: 8192 + (dt_ + 1) * T]
            if dt_ < 6:
                return t["gbc"][:, (dt_ - 4) * T:(dt_ - 3) * T]
            if dt_ == 6:
                return t["tsil"][:, 0:T]
            return t["tmpb"][:, :]
        if self.on("gp"):
            self.wt("gp", ("bmm", 0, E - 1, 1, NDT - 1))
            self.wt("gp", ("bev", E * 2 * NDT - 1))
            for dt_ in range(5, 8):
                self.wt("gp", ("pout", dt_))
        pins = [(t["w3b"][:, 8192:16384], 0, 4 * T),
                (t["gbc"][:, 0:2 * T], 4 * T, 2 * T),
                (t["tsil"][:, 0:T], 6 * T, T),
                (t["tmpb"][:, :], 7 * T, T)]
        for i, (dst, off, w) in enumerate(pins):
            inst = self.q.dma_start(dst, t["d_P"][:, off:off + w]) if self.on("gp") else None
            self.dmark("d_pin", inst, ("pin", i))

        for dt_ in VE_DTS:
            if dt_ > 0:
                self.wt("ve", ("yx", dt_ - 1))
            self.bx_into(hf2, dt_, mark_key=("bx1", dt_))
            if self.on("ve"):
                dec = bass.AP(t["ah_s"], dt_, [[8, 128], [0, T]])
                isc = self.q.tensor_tensor_scan(hf3[:, :], dec, hf2[:, :], 0.0, ALU.mult, ALU.add)
                self.sep(isc)
            else:
                isc = None
            self.mark("ve", isc, ("s1m", dt_))
            self.wt("sc", ("s1m", dt_))
            inst = self.q.activation(t["S_all"][:, dt_:dt_ + 1], hf3[:, T - 1:T], AF.Copy) if self.on("sc") else None
            self.mark("sc", inst, ("scol", dt_))
            inst = self.q.activation(t["xn"][:, dt_ * T:(dt_ + 1) * T], hf3[:, :], AF.Copy) if self.on("sc") else None
            self.mark("sc", inst, ("yx", dt_))

        self.wt("sy", ("scol", NDT - 1))
        inst = self.q.dma_start(t["d_agin"][:, :], t["S_all"][:, :]) if self.on("sy") else None
        self.dmark("d_ag1", inst, "ag1")
        self.wout_stream()
        if self.on("gp"):
            self.wt("gp", "ag1")
        if self.on("gp"):
            self.q.collective_compute(
                "AllGather", ALU.bypass,
                replica_groups=[[0, 1, 2, 3, 4, 5, 6, 7]],
                ins=[t["d_agin"].ap().opt()],
                outs=[t["d_agout"].ap().opt()],
            ).then_inc(self.S["s_cc"])
            self.q.wait_ge(self.S["s_cc"], 2)
        sg_src = bass.AP(t["d_agout"], 0, [[8, 128], [1024, 8], [1, 8]])
        inst = self.q.dma_start(t["Sg_all"][:, :].rearrange("p (j k) -> p j k", k=8), sg_src) if self.on("gp") else None
        self.dmark("d_ag2", inst, "ag2")

        if self.on("ve"):
            v = self.q
            self.wt("ve", "ag2")
            v.tensor_tensor(t["Sg_all"][:, :], t["Sg_all"][:, :], t["w8_s"][:, :], ALU.mult)
            v.drain()
            v.tensor_reduce(t["carry"][:, :], bass.AP(t["Sg_all"], 0, [[64, 128], [1, 8], [8, 8]]),
                            AXX, ALU.add)
            v.drain()
            inst = v.engine_nop()
        else:
            inst = None
        self.mark("ve", inst, "carry")

        for dt_ in VE_DTS:
            self.wt("ve", ("yx", dt_))
            if dt_ == 0:
                self.wt("ve", ("pin", 3))
            if self.on("ve"):
                xs = t["xn"][:, dt_ * T:(dt_ + 1) * T]
                inst = self.q.scalar_tensor_tensor(xs, pslot(dt_), t["carry"][:, dt_:dt_ + 1],
                                                   xs, ALU.mult, ALU.add)
            else:
                inst = None
            self.mark("ve", inst, ("yb", dt_))
        self.M["ybve"] = self.M[("yb", NDT - 1)]
        self.M["ybgp"] = self.M["ybve"]

        banks = [(t["pxA"], 0), (t["pxA"], 512), (t["pxB"], 0), (t["pxB"], 512)]
        for b in range(NDT):
            if self.on("pe"):
                if b == 0:
                    self.wt("pe", "wo")
                    self.wt("pe", ("bmul", E * 2 * NDT - 1))
                    self.wt("pe", ("mbrb", tag))
                for dc in range(NDT):
                    self.wt("pe", ("yb", dc))
                    for gi in range(4):
                        g = b * 4 + gi
                        et, tg = g // 4, g % 4
                        px, off = banks[g % 4]
                        if dc == 0 and g >= 4:
                            self.wt("pe", ("wev", (g - 4) // 4, (g - 4) % 4))
                        l = t["w3b"][:, dc * 1024 + et * 128: dc * 1024 + (et + 1) * 128]
                        rhs = t["xn"][:, dc * T + tg * 512: dc * T + tg * 512 + 512]
                        inst = self.q.matmul(px[:, off:off + 512], l, rhs,
                                             start=(dc == 0), stop=(dc == NDT - 1))
                        if dc == NDT - 1:
                            self.mark("pe", inst, ("wmm", et, tg))
            else:
                for gi in range(4):
                    self.mark("pe", None, ("wmm", b, gi))
            for gi in range(4):
                et, tg = b, gi
                px, off = banks[gi]
                self.wt("ve", ("wmm", et, tg))
                if self.on("ve"):
                    xs = t["xT"][:, et * T + tg * 512: et * T + tg * 512 + 512]
                    inst = self.q.scalar_tensor_tensor(xs, px[:, off:off + 512],
                                                       t["bo_s"][:, et:et + 1], xs, ALU.add, ALU.add)
                else:
                    inst = None
                self.mark("ve", inst, ("wev", et, tg))
                self.dr("ve")

    def bx_into(self, dst, dt_, mark_key=None):
        t = self.t
        v = self.q
        if self.on("ve"):
            i1 = v.tensor_tensor(dst[:, :], t["xT"][:, dt_ * T:(dt_ + 1) * T], t["mbf"][:, :], ALU.subtract)
            self.sep(i1)
            i2 = v.scalar_tensor_tensor(dst[:, :], dst[:, :], t["bv_s"][:, dt_:dt_ + 1],
                                        t["rbf"][:, :], ALU.mult, ALU.mult)
        else:
            i2 = None
        if mark_key is not None:
            self.mark("ve", i2, mark_key)
        if self.on("ve"):
            self.sep(i2)

    def output(self):
        t = self.t
        for dt_ in range(NDT):
            self.wt("sy", ("bev", ((E + E - 1) * 2 + 1) * NDT + dt_))
            inst = self.q.dma_start(t["d_out"][dt_ * 128:(dt_ + 1) * 128, :],
                                    t["xT"][:, dt_ * T:(dt_ + 1) * T]) if self.on("sy") else None
            self.dmark("d_out", inst)
        if self.on("sy"):
            self.q.wait_ge(self.S["d_out"], self.d["d_out"])


def build_nc():
    nc = bass.Bass("TRN2", target_bir_lowering=False)
    tn = {}
    d = tn
    d["d_x"] = nc.dram_tensor("x_t", [D, T], F32, kind="ExternalInput")
    d["d_w1"] = nc.dram_tensor("w1t", [E * D, FF], F32, kind="ExternalInput")
    d["d_w2"] = nc.dram_tensor("w2t", [E * D, FF], F32, kind="ExternalInput")
    d["d_w3"] = nc.dram_tensor("w3t", [E * FF, D], F32, kind="ExternalInput")
    d["d_wg"] = nc.dram_tensor("wgt", [D, E], F32, kind="ExternalInput")
    d["d_wo"] = nc.dram_tensor("wot", [D, D], F32, kind="ExternalInput")
    d["d_av"] = nc.dram_tensor("av", [128, 8], F32, kind="ExternalInput")
    d["d_bv"] = nc.dram_tensor("bv", [128, 8], F32, kind="ExternalInput")
    d["d_bo"] = nc.dram_tensor("bo", [128, 8], F32, kind="ExternalInput")
    d["d_ident"] = nc.dram_tensor("identc", [128, 128], F32, kind="ExternalInput")
    d["d_iota"] = nc.dram_tensor("iotac", [128, 128], F32, kind="ExternalInput")
    d["d_gmask"] = nc.dram_tensor("gmaskc", [128, 24], F32, kind="ExternalInput")
    d["d_onesr"] = nc.dram_tensor("onesrc", [1, 128], F32, kind="ExternalInput")
    d["d_onesc"] = nc.dram_tensor("onescc", [128, 1], F32, kind="ExternalInput")
    d["d_eps"] = nc.dram_tensor("epsc", [128, 1], F32, kind="ExternalInput")
    d["d_out"] = nc.dram_tensor("out_t", [D, T], F32, kind="ExternalOutput")
    d["d_agin"] = nc.dram_tensor("ag_in", [1, 1024], F32)
    d["d_gfm"] = nc.dram_tensor("g_fm_d", [8, 2048], F32)
    d["d_agout"] = nc.dram_tensor("ag_out", [8, 1024], F32)
    d["d_wuin"] = nc.dram_tensor("ag_wu_in", [1, 1024], F32)
    d["d_wuout"] = nc.dram_tensor("ag_wu_out", [8, 1024], F32)
    d["d_P"] = nc.dram_tensor("p_pow_d", [128, NDT * T], BF16)

    A = nc.alloc_sbuf_tensor
    d["xT"] = A("xT", [128, NDT * T], F32)
    d["xn"] = A("xn", [128, NDT * T], BF16)
    d["hid"] = A("hid", [128, 16384], BF16)
    d["w3b"] = A("w3b", [128, 16384], BF16)
    d["w1r"] = A("w1r", [128, 4096], BF16)
    d["w2r"] = A("w2r", [128, 4096], BF16)
    d["tsil"] = A("tsil", [128, 4096], BF16)
    d["gbc"] = A("gbc", [128, 4096], BF16)
    d["tmpb"] = A("tmpb", [128, 2048], BF16)
    d["sc_tm"] = A("sc_tm", [128, 128], F32)
    d["g_tm"] = d["sc_tm"]
    d["g_blk"] = A("g_blk", [8, 512], F32)
    d["wk0"] = A("wk0", [128, 128], F32)
    d["wk1"] = A("wk1", [128, 128], F32)
    d["mk1"] = A("mk1", [128, 128], F32)
    d["mk2"] = A("mk2", [128, 128], F32)
    d["ident_s"] = A("ident_s", [128, 128], F32)
    d["iota_s"] = A("iota_s", [128, 128], F32)
    d["gm_s"] = A("gm_s", [128, 24], F32)
    d["onesr_s"] = A("onesr_s", [1, 128], F32)
    d["eps_s"] = A("eps_s", [128, 1], F32)
    d["onescol_bf"] = A("onescol_bf", [128, 1], BF16)
    d["wg_bf"] = A("wg_bf", [128, 64], BF16)
    d["av_s"] = A("av_s", [128, 8], F32)
    d["bv_s"] = A("bv_s", [128, 8], F32)
    d["bo_s"] = A("bo_s", [128, 8], F32)
    d["ah_s"] = A("ah_s", [128, 8], F32)
    d["abig_s"] = A("abig_s", [128, 8], F32)
    d["a2_s"] = A("a2_s", [128, 8], F32)
    d["S_all"] = A("S_all", [128, 8], F32)
    d["Sg_all"] = A("Sg_all", [128, 64], F32)
    d["carry"] = A("carry", [128, 8], F32)
    d["wc8"] = A("wc8", [128, 8], F32)
    d["w8_s"] = A("w8_s", [128, 64], F32)
    d["onescf"] = A("onescf", [128, 1], F32)
    d["m1"] = A("m1", [128, 16], F32)
    d["m2"] = A("m2", [128, 16], F32)
    d["i1"] = A("i1", [128, 16], F32)
    d["i2"] = A("i2", [128, 16], F32)
    d["p1"] = A("p1", [128, 16], F32)
    d["p2"] = A("p2", [128, 16], F32)
    d["d12"] = A("d12", [128, 16], F32)

    d["ph1"] = nc.alloc_psum_tensor("ph1", [128, 1024], F32)
    d["ph2"] = nc.alloc_psum_tensor("ph2", [128, 1024], F32)
    d["pxA"] = nc.alloc_psum_tensor("pxA", [128, 1024], F32)
    d["pxB"] = nc.alloc_psum_tensor("pxB", [128, 1024], F32)

    hidf = d["hid"].ap().bitcast(F32)
    d["mbf"] = hidf[:, 0 * T:1 * T]
    d["rbf"] = hidf[:, 1 * T:2 * T]
    d["hf2"] = hidf[:, 2 * T:3 * T]
    d["hf3"] = hidf[:, 3 * T:4 * T]
    xnf = d["xn"].ap().bitcast(F32)
    d["sums_r"] = xnf[0:1, 0:T]
    d["sumsq_r"] = xnf[0:1, T:2 * T]

    sems = {}
    names = (["s_pe", "s_ve", "s_sc", "s_gp", "s_sy", "s_cc"] +
             ["d_in", "d_wc", "d_w3", "d_wo", "d_ag1", "d_ag2", "d_out"] +
             ["w1_0", "w1_1", "w2_0", "w2_1", "d_gb0", "d_gb1", "d_gf0", "d_gf1", "d_gf2", "d_gf3", "s_gpc", "d_agz"] + ["d_po%d" % i for i in range(4)] + ["d_pin"] + ["d_x%d" % i for i in range(8)])
    for nm in names:
        sems[nm] = nc.alloc_semaphore(nm)

    with nc.Block() as block:

        @block.tensor
        def _(q):
            Walk("pe", q, nc, tn, sems).run()

        @block.vector
        def _(q):
            Walk("ve", q, nc, tn, sems).run()

        @block.scalar
        def _(q):
            Walk("sc", q, nc, tn, sems).run()

        @block.gpsimd
        def _(q):
            Walk("gp", q, nc, tn, sems).run()

        @block.sync
        def _(q):
            Walk("sy", q, nc, tn, sems).run()

    return nc


_NC_CACHE = None


def _get_nc():
    global _NC_CACHE
    if _NC_CACHE is None:
        _NC_CACHE = build_nc()
    return _NC_CACHE


def kernel(L0_attn_x, L0_attn_a, L0_attn_b, L0_attn_w_out, L0_attn_b_out,
           L0_ffn_Wg, L0_ffn_W1, L0_ffn_W2, L0_ffn_W3):
    x = np.asarray(L0_attn_x, dtype=np.float32)
    a = np.asarray(L0_attn_a, dtype=np.float32)
    b = np.asarray(L0_attn_b, dtype=np.float32)
    w_out = np.asarray(L0_attn_w_out, dtype=np.float32)
    b_out = np.asarray(L0_attn_b_out, dtype=np.float32)
    Wg = np.asarray(L0_ffn_Wg, dtype=np.float32)
    W1 = np.asarray(L0_ffn_W1, dtype=np.float32)
    W2 = np.asarray(L0_ffn_W2, dtype=np.float32)
    W3 = np.asarray(L0_ffn_W3, dtype=np.float32)

    w1t = np.ascontiguousarray(W1.transpose(0, 2, 1)).reshape(E * D, FF)
    w2t = np.ascontiguousarray(W2.transpose(0, 2, 1)).reshape(E * D, FF)
    w3t = np.ascontiguousarray(W3.transpose(0, 2, 1)).reshape(E * FF, D)
    wgt = np.ascontiguousarray(Wg.T)
    wot = np.ascontiguousarray(w_out.T)
    av = np.ascontiguousarray(a.reshape(8, 128).T)
    bv = np.ascontiguousarray(b.reshape(8, 128).T)
    bo = np.ascontiguousarray(b_out.reshape(8, 128).T)

    identc = np.eye(128, dtype=np.float32)
    iotac = np.ascontiguousarray(
        np.broadcast_to(np.tile(np.arange(8, dtype=np.float32), 16)[None, :], (128, 128)))
    onesrc = np.ones((1, 128), dtype=np.float32)
    onescc = np.ones((128, 1), dtype=np.float32)
    epsc = np.full((128, 1), EPS, dtype=np.float32)

    xs = x.reshape(2, 4, T, D)
    in_maps = []
    for c in range(8):
        bb, kk = c // 4, c % 4
        gm = np.zeros((128, 24), dtype=np.float32)
        for ppow in range(3):
            j = kk - 1 - ppow
            if j >= 0:
                gm[:, ppow * 8 + (4 * bb + j)] = 1.0
        in_maps.append({
            "x_t": np.ascontiguousarray(xs[bb, kk].T),
            "w1t": w1t, "w2t": w2t, "w3t": w3t, "wgt": wgt, "wot": wot,
            "av": av, "bv": bv, "bo": bo,
            "identc": identc, "iotac": iotac,
            "gmaskc": gm, "onesrc": onesrc, "onescc": onescc,
            "epsc": epsc,
        })

    nc = _get_nc()
    res = run_bass_kernel_spmd(nc, in_maps, core_ids=list(range(8)))
    out = np.empty((2, 8192, D), dtype=np.float32)
    ov = out.reshape(2, 4, T, D)
    for c in range(8):
        ov[c // 4, c % 4] = res.results[c]["out_t"].T
    return out
